# version 04
# baseline (speedup 1.0000x reference)
import numpy as np
import ml_dtypes
import concourse.bass as bass
import concourse.mybir as mybir
from concourse.bass_utils import run_bass_kernel_spmd

F32 = mybir.dt.float32
BF16 = mybir.dt.bfloat16
I32 = mybir.dt.int32
ALU = mybir.AluOpType
AF = mybir.ActivationFunctionType
AX = mybir.AxisListType

MAGIC = 12582912.0
NEG = -1.0e30
T = 2048
D = 1024
DC = 8
H = 4096
CC = 32
HS = 128
STRIP = 256
NSTRIP = T // STRIP
SL = STRIP + 2
QC = 256
NQC = T // QC
NCORE = 8
import os
KPHASE = int(os.environ.get("KPHASE", "99"))

ENGINES = ("sync", "vector", "scalar", "tensor", "gpsimd")

SC64 = np.float64(128.0) ** -0.5
SC_HI = float(np.float32(SC64))
SC_LO = float(np.float32(SC64 - np.float64(np.float32(SC64))))


class Prog:
    def __init__(self, nc):
        self.nc = nc
        self.ctx = []
        self.handles = {}
        self.counts = {}
        self.steps = {e: [] for e in ENGINES}

    def _h(self, name):
        if name not in self.handles:
            cm = self.nc.semaphore(name)
            self.handles[name] = cm.__enter__()
            self.ctx.append(cm)
        return self.handles[name]

    def step(self, eng, fn, deps=(), sig=None, inc=1):
        deps = tuple(d for d in deps if d is not None)
        after = None
        if sig is not None:
            self.counts[sig] = self.counts.get(sig, 0) + inc
            after = (sig, self.counts[sig], inc)
        self.steps[eng].append((deps, fn, after))
        return (sig, self.counts[sig]) if sig else None

    def dma(self, eng, qsem, dst, src, deps=()):
        return self.step(eng, lambda e: e.dma_start(out=dst, in_=src),
                         deps=deps, sig=qsem, inc=16)

    def barrier_ev(self, sig):
        return (sig, self.counts.get(sig, 0))

    def check_cycles(self):
        sig_step = {}
        nodes = []
        for eng in ENGINES:
            for i, (deps, fn, after) in enumerate(self.steps[eng]):
                nid = (eng, i)
                nodes.append(nid)
                if after is not None:
                    sn, val, n = after
                    lo = val - n
                    for v in range(lo + 1, val + 1):
                        sig_step[(sn, v)] = nid
        adj = {n: [] for n in nodes}
        indeg = {n: 0 for n in nodes}
        for eng in ENGINES:
            for i, (deps, fn, after) in enumerate(self.steps[eng]):
                nid = (eng, i)
                if i > 0:
                    adj[(eng, i - 1)].append(nid)
                    indeg[nid] += 1
                for (sn, val) in deps:
                    src_n = sig_step.get((sn, val))
                    if src_n is not None and src_n != nid:
                        adj[src_n].append(nid)
                        indeg[nid] += 1
        import collections
        q = collections.deque(n for n in nodes if indeg[n] == 0)
        seen = 0
        while q:
            n = q.popleft()
            seen += 1
            for m in adj[n]:
                indeg[m] -= 1
                if indeg[m] == 0:
                    q.append(m)
        if seen != len(nodes):
            stuck = [n for n in nodes if indeg[n] > 0]
            raise RuntimeError(f"dependency cycle: {len(nodes)-seen} stuck; "
                               f"first stuck: {stuck[:6]}")

    def build(self, blk):
        prog = self
        self.check_cycles()

        def make(engname):
            def body(eng):
                seen = {}
                for deps, fn, after in prog.steps[engname]:
                    for (sn, val) in deps:
                        if val <= seen.get(sn, 0):
                            continue
                        eng.wait_ge(prog._h(sn), val)
                        seen[sn] = val
                    ins = fn(eng)
                    if after is not None:
                        sn, val, n = after
                        ins.then_inc(prog._h(sn), n)
            return body

        for e in ENGINES:
            if self.steps[e]:
                getattr(blk, e)(make(e))


BASE = 17920
TOP = 229340

_STATIC = [
    ("rows_stats", [1, 4096], F32),
    ("rows_tmp", [1, 1536], F32),
    ("qisc", [128, 4, SL], I32),
    ("eyef", [128, 128], F32),
    ("rotm", [128, 128], F32),
    ("ones128", [1, 128], F32),
    ("ones128b", [1, 128], BF16),
    ("onesc", [128, 1], F32),
    ("onescb", [128, 1], BF16),
    ("colA", [128, 16], F32),
    ("srow", [1, 1280], F32),
    ("cw_s", [128, CC, 3], F32),
    ("cb_s", [128, CC], F32),
]
_ERA_A = [
    ("xn", [128, DC, T], BF16),
    ("xbuf", [128, 2, DC, QC], F32),
    ("cosf", [128, T], F32),
    ("sinf", [128, T], F32),
    ("sqscr", [128, T], F32),
    ("wraw", [128, 2, 1024], F32),
    ("wqf", [128, 2048], F32),
    ("wqc", [128, 2048], F32),
    ("wq_s", [128, 2, DC, HS], BF16),
    ("wk_s", [128, DC, HS], BF16),
    ("wv_s", [128, DC, HS], BF16),
    ("wo_s", [128, 2, D], BF16),
    ("kb", [128, T], BF16),
    ("qb", [128, 2, T], BF16),
    ("vtm", [128, 16, HS], BF16),
    ("probs", [128, 16, QC], BF16),
    ("masks", [128, 2, QC], BF16),
    ("ao_n", [128, 2, QC], BF16),
    ("w2stage", [128, 1024], BF16),
    ("bcA", [128, QC], F32),
    ("bcA2", [128, QC], F32),
    ("subt", [128, QC], F32),
    ("kqf", [128, T], F32),
]
_ERA_F = [
    ("w1q", [128, DC, H], BF16),
    ("hbuf", [128, CC, SL], F32),
    ("x2a", [128, DC, SL], F32),
    ("xn2", [128, DC, SL], F32),
    ("fscr", [128, 8, SL], F32),
    ("xq2", [128, CC, 2 * STRIP], BF16),
    ("xqs", [128, DC, SL], BF16),
    ("bcB", [128, SL], F32),
    ("cscr", [128, 2, SL], F32),
    ("mxr", [128, SL], F32),
]
_ERA_W = [
    ("w2qb", [128, 2, 16, 128], BF16),
    ("outb", [128, 2, 512], F32),
    ("tscr", [128, 512], F32),
    ("bcC", [128, 2 * STRIP], F32),
]


def _nbytes(shape, dt):
    n = 1
    for s in shape[1:]:
        n *= s
    return n * {F32: 4, BF16: 2, I32: 4}[dt]


def plan_memory():
    plan = {}
    off = BASE
    for name, shape, dt in _STATIC:
        plan[name] = (off, shape, dt)
        off += (_nbytes(shape, dt) + 31) // 32 * 32
    shared = off
    for era in (_ERA_A, _ERA_F):
        o = shared
        for name, shape, dt in era:
            plan[name] = (o, shape, dt)
            o += (_nbytes(shape, dt) + 31) // 32 * 32
        if o > TOP:
            raise RuntimeError(f"sbuf era overflow by {o - TOP} bytes")
    fend = max(plan[n][0] + _nbytes(s, d) for n, s, d in _ERA_F)
    o = (fend + 31) // 32 * 32
    for name, shape, dt in _ERA_W:
        if name in ("w2qb", "outb"):
            plan[name] = (o, shape, dt)
            o += (_nbytes(shape, dt) + 31) // 32 * 32
    if o > TOP:
        raise RuntimeError(f"w-era tail overflow by {o - TOP} bytes")
    o2 = plan["fscr"][0]
    for name, shape, dt in _ERA_W:
        if name in ("tscr", "bcC"):
            plan[name] = (o2, shape, dt)
            o2 += (_nbytes(shape, dt) + 31) // 32 * 32
    fscr_end = plan["fscr"][0] + _nbytes([128, 8, SL], F32)
    if o2 > fscr_end:
        raise RuntimeError(f"tscr/bcC overflow fscr by {o2 - fscr_end} bytes")
    return plan


def build_program():
    nc = bass.Bass("TRN2", target_bir_lowering=False, debug=False)
    hbm = dict(
        X=nc.dram_tensor("x", [128, DC, T], F32, kind="ExternalInput").ap(),
        WQ=nc.dram_tensor("wq", [128, 2 * DC * HS], F32, kind="ExternalInput").ap(),
        WK=nc.dram_tensor("wk", [128, DC * HS], F32, kind="ExternalInput").ap(),
        WV=nc.dram_tensor("wv", [128, DC * HS], F32, kind="ExternalInput").ap(),
        WO=nc.dram_tensor("wo", [128, 2 * D], F32, kind="ExternalInput").ap(),
        W1T=nc.dram_tensor("w1t", [128, DC, H], F32, kind="ExternalInput").ap(),
        W2T=nc.dram_tensor("w2t", [128, CC, D], F32, kind="ExternalInput").ap(),
        CWB=nc.dram_tensor("cwb", [128, 4 * CC], F32, kind="ExternalInput").ap(),
        COS=nc.dram_tensor("cosin", [128, 2, T], F32, kind="ExternalInput").ap(),
        MSK=nc.dram_tensor("msk", [128, 2, QC], BF16, kind="ExternalInput").ap(),
        EYE=nc.dram_tensor("eyer", [128, 2, 128], F32, kind="ExternalInput").ap(),
        OUT=nc.dram_tensor("out", [128, DC, T], F32, kind="ExternalOutput").ap(),
        X2S=nc.dram_tensor("x2spill", [128, DC, T], F32, kind="ExternalOutput").ap(),
        W2Q=nc.dram_tensor("w2qs", [DC, 128, CC * 128], BF16, kind="ExternalOutput").ap(),
    )
    plan = plan_memory()
    bufs = {}
    import contextlib
    stk = contextlib.ExitStack()
    with stk:
        for name, (off, shape, dt) in plan.items():
            bufs[name] = nc.alloc_sbuf_tensor_at(name, shape, dt, offset=off).ap()
        pbank = [stk.enter_context(nc.psum_tensor(f"pb{i}", [128, 512], F32)).ap()
                 for i in range(8)]
        p = Prog(nc)
        with nc.Block() as blk:
            emit(p, bufs, pbank, hbm)
            for cm in p.ctx:
                stk.push(cm)
            p.build(blk)
    return nc


def emit(p, b, pb, hbm):
    V, S, Tn, G, SY = "vector", "scalar", "tensor", "gpsimd", "sync"
    X, WQ, WK, WV, WO = hbm["X"], hbm["WQ"], hbm["WK"], hbm["WV"], hbm["WO"]
    W1T, W2T, CWB = hbm["W1T"], hbm["W2T"], hbm["CWB"]
    COS, MSK, EYE, OUT = hbm["COS"], hbm["MSK"], hbm["EYE"], hbm["OUT"]
    X2S, W2Q = hbm["X2S"], hbm["W2Q"]

    rows_stats, rows_tmp = b["rows_stats"], b["rows_tmp"]
    qisc, eyef, rotm = b["qisc"], b["eyef"], b["rotm"]
    ones128, ones128b, colA, srow = b["ones128"], b["ones128b"], b["colA"], b["srow"]
    onesc, onescb = b["onesc"], b["onescb"]
    cw_s, cb_s = b["cw_s"], b["cb_s"]
    xn, xbuf = b["xn"], b["xbuf"]
    cosf, sinf, sqscr = b["cosf"], b["sinf"], b["sqscr"]
    wraw, wqf, wqc = b["wraw"], b["wqf"], b["wqc"]
    wq_s, wk_s, wv_s, wo_s = b["wq_s"], b["wk_s"], b["wv_s"], b["wo_s"]
    kb, qb, vtm = b["kb"], b["qb"], b["vtm"]
    probs, masks, ao_n = b["probs"], b["masks"], b["ao_n"]
    w2stage, bcA, bcA2, subt, kqf = b["w2stage"], b["bcA"], b["bcA2"], b["subt"], b["kqf"]
    w1q, hbuf, x2a, xn2 = b["w1q"], b["hbuf"], b["x2a"], b["xn2"]
    fscr, xq2, xqs, w2qb = b["fscr"], b["xq2"], b["xqs"], b["w2qb"]
    bcB, outb, tscr, cscr, mxr = (b["bcB"], b["outb"], b["tscr"],
                                  b["cscr"], b["mxr"])
    bcC = b["bcC"]

    def debug_dump(pairs, deps):
        evd = None
        for i, (dst_c, srcap) in enumerate(pairs):
            dst = OUT[:, dst_c, :] if isinstance(dst_c, int) else dst_c
            evd = p.dma(G, "qg", dst, srcap, deps=deps)
        p.step(SY, lambda s: s.nop(), deps=[evd], sig=None)

    ev_cos = p.dma(SY, "qs", cosf[:, :], COS[:, 0, :])
    ev_sin = p.dma(SY, "qs", sinf[:, :], COS[:, 1, :])
    ev_msk = p.dma(SY, "qs", masks[:, :, :], MSK)
    ev_eye = p.dma(SY, "qs", eyef[:, :], EYE[:, 0, :])
    ev_rot = p.dma(SY, "qs", rotm[:, :], EYE[:, 1, :])
    ev_cwb = p.dma(SY, "qs", cw_s.rearrange("p a c -> p (a c)"), CWB[:, 0:3 * CC])
    ev_cbb = p.dma(SY, "qs", cb_s[:, :], CWB[:, 3 * CC:4 * CC])
    ev_o1 = p.step(V, lambda v: v.memset(ones128[:, :], 1.0), sig="v")
    ev_o1b = p.step(V, lambda v: v.memset(ones128b[:, :], 1.0), sig="v")
    ev_oc = p.step(V, lambda v: v.memset(onesc[:, :], 1.0), sig="v")
    ev_ocb = p.step(V, lambda v: v.memset(onescb[:, :], 1.0), sig="v")

    ev_wq = p.dma(SY, "qs", kqf[:, 0:2 * DC * HS], WQ)
    ev_wk = p.dma(SY, "qs", sqscr[:, 0:DC * HS], WK)
    ev_wv = p.dma(SY, "qs", sqscr[:, DC * HS:2 * DC * HS], WV)
    ev_wo = p.dma(SY, "qs", wqf[:, 0:2 * D], WO)
    e1 = p.step(V, lambda v: v.tensor_scalar(
        out=wqc[:, 0:2 * DC * HS], in0=kqf[:, 0:2 * DC * HS],
        scalar1=SC_HI, scalar2=None, op0=ALU.mult), deps=[ev_wq], sig="v")
    ev_wqs = p.step(V, lambda v: v.scalar_tensor_tensor(
        out=wq_s.rearrange("p a b c -> p (a b c)"),
        in0=kqf[:, 0:2 * DC * HS], scalar=SC_LO,
        in1=wqc[:, 0:2 * DC * HS], op0=ALU.mult, op1=ALU.add),
        deps=[e1], sig="v")
    ev_wks = p.step(V, lambda v: v.tensor_copy(
        out=wk_s.rearrange("p a c -> p (a c)"),
        in_=sqscr[:, 0:DC * HS]), deps=[ev_wk], sig="v")
    ev_wvs = p.step(V, lambda v: v.tensor_copy(
        out=wv_s.rearrange("p a c -> p (a c)"),
        in_=sqscr[:, DC * HS:2 * DC * HS]), deps=[ev_wv], sig="v")
    ev_wos = p.step(V, lambda v: v.tensor_copy(
        out=wo_s.rearrange("p a c -> p (a c)"), in_=wqf[:, 0:2 * D]),
        deps=[ev_wo], sig="v")

    last_red = [None, None]
    for wi, W in enumerate([W1T, W2T]):
        Wf = W.rearrange("p a c -> p (a c)")
        npieces = (DC * H) // 1024 if wi == 0 else (CC * D) // 1024
        for i in range(npieces):
            buf = wraw[:, i % 2, :]
            if wi == 0:
                deps = [last_red[0]] if i >= 2 else [ev_wos]
            else:
                deps = [last_red[1]] if i >= 2 else [last_red[0]]
            ev_l = p.dma(SY, "qs", buf, Wf[:, i * 1024:(i + 1) * 1024], deps=deps)
            ev_r = p.step(V, lambda v, buf=buf, wi=wi: v.tensor_reduce(
                out=colA[:, 4 + wi:5 + wi], in_=buf, axis=AX.X, op=ALU.add,
                apply_absolute_value=True), deps=[ev_l], sig="v")
            if i == 0:
                ev_a = p.step(V, lambda v, wi=wi: v.tensor_copy(
                    out=colA[:, wi:wi + 1], in_=colA[:, 4 + wi:5 + wi]),
                    deps=[ev_r], sig="v")
            else:
                ev_a = p.step(V, lambda v, wi=wi: v.tensor_add(
                    out=colA[:, wi:wi + 1], in0=colA[:, wi:wi + 1],
                    in1=colA[:, 4 + wi:5 + wi]), deps=[ev_r], sig="v")
            last_red[wi] = ev_a
    ev_wt = p.step(Tn, lambda t: t.matmul(
        pb[3][0:1, 0:2], onesc[:, :], colA[:, 0:2], start=True, stop=True),
        deps=[last_red[0], last_red[1], ev_oc], sig="t")
    ev_wm = p.step(V, lambda v: v.tensor_scalar(
        out=srow[0:1, 8:10], in0=pb[3][0:1, 0:2], scalar1=float(2.0 ** -22),
        scalar2=1e-5, op0=ALU.mult, op1=ALU.max), deps=[ev_wt], sig="v")
    ev_wr = p.step(V, lambda v: v.reciprocal(
        out=srow[0:1, 10:12], in_=srow[0:1, 8:10]), deps=[ev_wm], sig="v")
    ev_wb = p.step(Tn, lambda t: t.matmul(
        pb[3][:, 0:2], ones128[:, :], srow[0:1, 10:12], start=True, stop=True),
        deps=[ev_wr], sig="t")
    ev_wb2 = p.step(V, lambda v: v.tensor_copy(
        out=colA[:, 2:4], in_=pb[3][:, 0:2]), deps=[ev_wb], sig="v")
    ev_wsi = p.step(V, lambda v: v.reciprocal(
        out=colA[:, 6:8], in_=colA[:, 2:4]), deps=[ev_wb2], sig="v")

    xv = xbuf.rearrange("p s a q -> p s (a q)")
    ev_ch = []
    ev_sq = []
    for c in range(DC):
        deps = [ev_ch[c - 2]] if c >= 2 else []
        ev_l = p.dma(G, "qg", xv[:, c % 2, :], X[:, c, :], deps=deps)
        evq = p.step(S, lambda s, c=c: s.activation(
            out=sqscr[:, :], in_=xv[:, c % 2, :], func=AF.Square),
            deps=[ev_l, ev_wks, ev_wvs] + ([ev_ch[-1]] if ev_ch else []), sig="s")
        for blk in range(4):
            p.step(Tn, lambda t, c=c, blk=blk: t.matmul(
                pb[blk][0:1, 0:512], onesc[:, :],
                xv[:, c % 2, blk * 512:(blk + 1) * 512],
                start=(c == 0), stop=(c == DC - 1)),
                deps=[ev_l, ev_oc], sig=None)
            ev = p.step(Tn, lambda t, c=c, blk=blk: t.matmul(
                pb[4 + blk][0:1, 0:512], onesc[:, :],
                sqscr[:, blk * 512:(blk + 1) * 512],
                start=(c == 0), stop=(c == DC - 1)),
                deps=[evq], sig="t")
        ev_ch.append(ev)
        ev_sq.append(evq)
    ev_ln1mm = ev_ch[-1]
    ev_ev = []
    for blk in range(4):
        ev_ev.append(p.step(V, lambda v, blk=blk: v.tensor_copy(
            out=rows_stats[0:1, blk * 512:(blk + 1) * 512],
            in_=pb[blk][0:1, 0:512]), deps=[ev_ln1mm], sig="v"))
        ev_ev.append(p.step(V, lambda v, blk=blk: v.tensor_copy(
            out=rows_stats[0:1, T + blk * 512:T + (blk + 1) * 512],
            in_=pb[4 + blk][0:1, 0:512]), deps=[ev_ln1mm], sig="v"))
    ev_st = ev_ev[-1]

    def ln_stats_math(dep):
        out_evs = []
        for blk in range(4):
            sl = slice(blk * 512, (blk + 1) * 512)
            sl2 = slice(T + blk * 512, T + (blk + 1) * 512)
            e1 = p.step(V, lambda v, sl=sl: v.tensor_scalar(
                out=rows_stats[0:1, sl], in0=rows_stats[0:1, sl],
                scalar1=float(1.0 / 1024.0), scalar2=None, op0=ALU.mult),
                deps=[dep], sig="v")
            e2 = p.step(V, lambda v, sl2=sl2: v.tensor_scalar(
                out=rows_stats[0:1, sl2], in0=rows_stats[0:1, sl2],
                scalar1=float(1.0 / 1024.0), scalar2=None, op0=ALU.mult),
                deps=[dep], sig="v")
            e3 = p.step(V, lambda v, sl=sl: v.tensor_tensor(
                out=rows_tmp[0:1, 0:512], in0=rows_stats[0:1, sl],
                in1=rows_stats[0:1, sl], op=ALU.mult), deps=[e1], sig="v")
            e4 = p.step(V, lambda v, sl2=sl2: v.tensor_sub(
                out=rows_stats[0:1, sl2], in0=rows_stats[0:1, sl2],
                in1=rows_tmp[0:1, 0:512]), deps=[e2, e3], sig="v")
            e5 = p.step(V, lambda v, sl2=sl2: v.tensor_scalar(
                out=rows_stats[0:1, sl2], in0=rows_stats[0:1, sl2],
                scalar1=1e-5, scalar2=None, op0=ALU.add), deps=[e4], sig="v")
            e6 = p.step(S, lambda s, sl2=sl2: s.activation(
                out=rows_tmp[0:1, 0:512], in_=rows_stats[0:1, sl2],
                func=AF.Sqrt), deps=[e5], sig="s")
            e7 = p.step(V, lambda v: v.reciprocal(
                out=rows_tmp[0:1, 512:1024], in_=rows_tmp[0:1, 0:512]),
                deps=[e6], sig="v")
            e8 = p.step(V, lambda v: v.tensor_tensor(
                out=rows_tmp[0:1, 1024:1536], in0=rows_tmp[0:1, 512:1024],
                in1=rows_tmp[0:1, 512:1024], op=ALU.mult), deps=[e7], sig="v")
            e9 = p.step(V, lambda v, sl2=sl2: v.tensor_tensor(
                out=rows_tmp[0:1, 1024:1536], in0=rows_tmp[0:1, 1024:1536],
                in1=rows_stats[0:1, sl2], op=ALU.mult), deps=[e8, e5], sig="v")
            e10 = p.step(V, lambda v: v.tensor_scalar(
                out=rows_tmp[0:1, 1024:1536], in0=rows_tmp[0:1, 1024:1536],
                scalar1=-0.5, scalar2=1.5, op0=ALU.mult, op1=ALU.add),
                deps=[e9], sig="v")
            e11 = p.step(V, lambda v, sl2=sl2: v.tensor_tensor(
                out=rows_stats[0:1, sl2], in0=rows_tmp[0:1, 512:1024],
                in1=rows_tmp[0:1, 1024:1536], op=ALU.mult), deps=[e10], sig="v")
            out_evs.append(e11)
        return out_evs[-1]

    ev_ln1 = ln_stats_math(ev_st)

    ev_ap = []
    prev_em = [ev_ln1]
    for c in range(DC):
        deps = [ev_ap[c - 2]] if c >= 2 else [ev_ln1]
        ev_l = p.dma(G, "qg", xv[:, c % 2, :], X[:, c, :], deps=deps)
        for blk in range(4):
            sl = slice(blk * 512, (blk + 1) * 512)
            em0 = p.step(Tn, lambda t, sl=sl: t.matmul(
                pb[0][:, 0:512], ones128[:, :], rows_stats[0:1, sl],
                start=True, stop=True), deps=[ev_ln1] + prev_em, sig="t")
            em1 = p.step(Tn, lambda t, sl=sl: t.matmul(
                pb[1][:, 0:512], ones128[:, :],
                rows_stats[0:1, T + sl.start:T + sl.stop],
                start=True, stop=True), deps=[ev_ln1] + prev_em, sig="t")
            es = p.step(V, lambda v, c=c, sl=sl: v.tensor_sub(
                out=sqscr[:, 0:512], in0=xv[:, c % 2, sl], in1=pb[0][:, 0:512]),
                deps=[ev_l, em0], sig="v")
            em = p.step(V, lambda v, c=c, sl=sl: v.tensor_tensor(
                out=xn[:, c, sl], in0=sqscr[:, 0:512], in1=pb[1][:, 0:512],
                op=ALU.mult), deps=[es, em1], sig="v")
            prev_em = [em]
        ev_ap.append(em)
    ev_xn = ev_ap[-1]
    if KPHASE == 1:
        debug_dump([(c, xn[:, c, :]) for c in range(DC)], [ev_xn])
        return

    ev_v = []
    for j in range(16):
        for c in range(DC):
            ev = p.step(Tn, lambda t, j=j, c=c: t.matmul(
                pb[2][:, 0:128], xn[:, c, j * 128:(j + 1) * 128],
                wv_s[:, c, :], start=(c == 0), stop=(c == DC - 1)),
                deps=[ev_xn, ev_wvs] + ([ev_v[-1]] if ev_v else []), sig="t")
        ev_e = p.step(S, lambda s, j=j: s.copy(
            out=vtm[:, j, :], in_=pb[2][:, 0:128]), deps=[ev], sig="s")
        ev_v.append(ev_e)
    ev_vd = ev_v[-1]
    if KPHASE == 21:
        debug_dump([(3, vtm.rearrange("p a c -> p (a c)"))], [ev_vd])
        return

    def proj_rope(dst, wsb, ev_w, ev_prev):
        evs = []
        prev_ec2 = None
        for nb in range(4):
            sl = slice(nb * 512, (nb + 1) * 512)
            for c in range(DC):
                evm = p.step(Tn, lambda t, c=c, sl=sl: t.matmul(
                    pb[2][:, 0:512], wsb[:, c, :], xn[:, c, sl],
                    start=(c == 0), stop=(c == DC - 1)),
                    deps=[ev_xn, ev_w, ev_wqs]
                    + ([evs[-1]] if evs else ([ev_prev] if ev_prev else [])),
                    sig="t")
            eve = p.step(S, lambda s, sl=sl: s.copy(
                out=kqf[:, sl], in_=pb[2][:, 0:512]), deps=[evm], sig="s")
            evr = p.step(Tn, lambda t, sl=sl: t.matmul(
                pb[3][:, 0:512], rotm[:, :], kqf[:, sl],
                start=True, stop=True), deps=[eve, ev_rot, prev_ec2], sig="t")
            ec1 = p.step(V, lambda v, sl=sl: v.tensor_tensor(
                out=sqscr[:, 0:512], in0=kqf[:, sl], in1=cosf[:, sl],
                op=ALU.mult), deps=[eve, ev_cos], sig="v")
            ec2 = p.step(V, lambda v, sl=sl: v.tensor_tensor(
                out=sqscr[:, 512:1024], in0=pb[3][:, 0:512], in1=sinf[:, sl],
                op=ALU.mult), deps=[evr, ev_sin], sig="v")
            prev_ec2 = ec2
            ec3 = p.step(V, lambda v, sl=sl: v.tensor_add(
                out=dst[:, sl], in0=sqscr[:, 0:512], in1=sqscr[:, 512:1024]),
                deps=[ec1, ec2], sig="v")
            evs.append(ec3)
        return evs[-1]

    ev_k = proj_rope(kb, wk_s, ev_wks, ev_vd)
    if KPHASE == 22:
        debug_dump([(0, kb[:, :])], [ev_k])
        return
    ev_q0 = proj_rope(qb[:, 0, :], wq_s[:, 0, :, :], ev_wqs, ev_k)
    ev_q1 = proj_rope(qb[:, 1, :], wq_s[:, 1, :, :], ev_wqs, ev_q0)
    if KPHASE == 23:
        debug_dump([(0, kb[:, :]), (1, qb[:, 0, :]), (2, qb[:, 1, :])], [ev_q1])
        return

    def sq_norms(src_bf, dst_row, dep):
        evs = []
        for blk in range(4):
            sl = slice(blk * 512, (blk + 1) * 512)
            eq = p.step(S, lambda s, sl=sl: s.activation(
                out=sqscr[:, sl], in_=src_bf[:, sl], func=AF.Square),
                deps=[dep] + (evs[-1:] if evs else []), sig="s")
            ev = p.step(Tn, lambda t, sl=sl: t.matmul(
                pb[4][0:1, 0:512],
                onesc[:, :], sqscr[:, sl], start=True, stop=True),
                deps=[eq, ev_oc] + (evs[-1:] if evs else []), sig="t")
            ee = p.step(V, lambda v, sl=sl, dst_row=dst_row: v.tensor_copy(
                out=dst_row[0:1, sl], in_=pb[4][0:1, 0:512]), deps=[ev], sig="v")
            evs.append(ee)
        return evs[-1]

    ev_nk = sq_norms(kb, rows_stats, ev_q1)
    if KPHASE == 24:
        debug_dump([(OUT[0:1, 4, :], rows_stats[0:1, 0:T])], [ev_nk])
        return
    ev_km = p.step(V, lambda v: v.reduce_max(
        out=srow[0:1, 0:1], in_=rows_stats[0:1, 0:T], axis=AX.X),
        deps=[ev_nk], sig="v")
    ev_km2 = p.step(S, lambda s: s.activation(
        out=srow[0:1, 1:2], in_=srow[0:1, 0:1], func=AF.Sqrt), deps=[ev_km], sig="s")

    if KPHASE == 2:
        debug_dump([(0, kb[:, :]), (1, qb[:, 0, :]), (2, qb[:, 1, :]),
                    (3, vtm.rearrange("p a c -> p (a c)"))], [ev_km2, ev_vd])
        return

    ev_spill = [None] * NQC
    ev_resid_done = [None] * NQC
    last_scores_cons = None
    last_wo = None
    prev_pb7 = None
    prev_rs = [ev_km2]
    prev_aon = [None, None]
    prev_stat_evac = None
    prev_h_av = None
    prev_resid_banks = [None, None]
    for qc in range(NQC):
        ev_xr = p.dma(G, "qg", xbuf[:, qc % 2, :, :],
                      X[:, :, qc * QC:(qc + 1) * QC],
                      deps=[ev_spill[qc - 2]] if qc >= 2 else [ev_xn])
        jmax = 2 * (qc + 1)
        ev_aon = [None, None]
        for h in range(2):
            eqn = p.step(S, lambda s, h=h, qc=qc: s.activation(
                out=sqscr[:, 256:512], in_=qb[:, h, qc * QC:(qc + 1) * QC],
                func=AF.Square), deps=[ev_q1], sig="s")
            emn = p.step(Tn, lambda t: t.matmul(
                pb[4][0:1, 0:QC], onesc[:, :], sqscr[:, 256:512],
                start=True, stop=True), deps=[eqn, ev_oc] + prev_rs, sig="t")
            esq = p.step(S, lambda s: s.activation(
                out=srow[0:1, 320:320 + QC], in_=pb[4][0:1, 0:QC],
                func=AF.Sqrt), deps=[emn], sig="s")
            emu = p.step(V, lambda v: v.tensor_scalar(
                out=srow[0:1, 320:320 + QC], in0=srow[0:1, 320:320 + QC],
                scalar1=srow[0:1, 1:2], scalar2=None, op0=ALU.mult),
                deps=[esq, ev_km2], sig="v")
            prev_rs = [esq]
            ev_mb = p.step(Tn, lambda t: t.matmul(
                pb[7][:, 0:QC], ones128[:, :], srow[0:1, 320:320 + QC],
                start=True, stop=True),
                deps=[emu, prev_pb7, ev_o1], sig="t")
            ev_mbe = p.step(S, lambda s: s.copy(
                out=bcA[:, :], in_=pb[7][:, 0:QC]), deps=[ev_mb], sig="s")
            ev_exp = []
            for j in range(jmax):
                bank = j % 2
                evsc = p.step(Tn, lambda t, j=j, h=h, qc=qc, bank=bank: t.matmul(
                    pb[bank][:, 0:QC], kb[:, j * 128:(j + 1) * 128],
                    qb[:, h, qc * QC:(qc + 1) * QC], start=True, stop=True),
                    deps=[ev_q1, last_scores_cons]
                    + ([prev_resid_banks[j]] if (h == 0 and j < 2) else []),
                    sig="t")
                dep_mask = None
                if j >= 2 * qc:
                    dep_mask = p.step(V, lambda v, j=j, qc=qc, bank=bank: v.tensor_tensor(
                        out=pb[bank][:, 0:QC], in0=pb[bank][:, 0:QC],
                        in1=masks[:, j - 2 * qc, :], op=ALU.add),
                        deps=[evsc, ev_msk], sig="v")
                evsub = p.step(V, lambda v, bank=bank: v.tensor_sub(
                    out=subt[:, :], in0=pb[bank][:, 0:QC], in1=bcA[:, :]),
                    deps=[evsc, dep_mask, ev_mbe]
                    + (ev_exp[-1:] if ev_exp else []), sig="v")
                last_scores_cons = evsub
                eve = p.step(S, lambda s, j=j: s.activation(
                    out=probs[:, j, :], in_=subt[:, :], func=AF.Exp),
                    deps=[evsub, prev_h_av], sig="s")
                ev_exp.append(eve)
                p.step(Tn, lambda t, j=j, jmax=jmax: t.matmul(
                    pb[4][0:1, 0:QC], onescb[:, :], probs[:, j, :],
                    start=(j == 0), stop=(j == jmax - 1)),
                    deps=[eve, ev_ocb] + (prev_rs if j == 0 else []), sig=None)
                evAV = p.step(Tn, lambda t, j=j, h=h, jmax=jmax: t.matmul(
                    pb[5 + h][:, 0:QC], vtm[:, j, :], probs[:, j, :],
                    start=(j == 0), stop=(j == jmax - 1)),
                    deps=[eve, ev_vd]
                    + ([prev_aon[h]] if j == 0 else []), sig="t")
            ev_rs = p.step(V, lambda v: v.reciprocal(
                out=srow[0:1, 64:64 + QC], in_=pb[4][0:1, 0:QC]),
                deps=[evAV], sig="v")
            prev_rs = [ev_rs]
            ev_rb = p.step(Tn, lambda t: t.matmul(
                pb[7][:, 0:QC], ones128[:, :], srow[0:1, 64:64 + QC],
                start=True, stop=True), deps=[ev_rs, ev_mbe], sig="t")
            ev_rbe = p.step(S, lambda s: s.copy(
                out=bcA2[:, :], in_=pb[7][:, 0:QC]), deps=[ev_rb], sig="s")
            prev_pb7 = ev_rbe
            ev_aon[h] = p.step(V, lambda v, h=h: v.tensor_tensor(
                out=ao_n[:, h, :], in0=pb[5 + h][:, 0:QC], in1=bcA2[:, :],
                op=ALU.mult), deps=[ev_rbe, evAV], sig="v")
            prev_aon[h] = ev_aon[h]
            prev_h_av = evAV
        ev_res = []
        for dc in range(DC):
            bank = dc % 2
            for hh in range(2):
                evw = p.step(Tn, lambda t, dc=dc, hh=hh, bank=bank: t.matmul(
                    pb[bank][:, 0:QC], wo_s[:, hh, dc * 128:(dc + 1) * 128],
                    ao_n[:, hh, :], start=(hh == 0), stop=(hh == 1)),
                    deps=[ev_aon[0], ev_aon[1], ev_wos,
                          ev_res[dc - 2] if dc >= 2 else last_wo], sig="t")
            evr = p.step(V, lambda v, dc=dc, qc=qc, bank=bank: v.tensor_tensor(
                out=xbuf[:, qc % 2, dc, :], in0=xbuf[:, qc % 2, dc, :],
                in1=pb[bank][:, 0:QC], op=ALU.add), deps=[evw, ev_xr], sig="v")
            ev_res.append(evr)
            evq2 = p.step(S, lambda s, dc=dc, qc=qc: s.activation(
                out=sqscr[:, 0:QC], in_=xbuf[:, qc % 2, dc, :], func=AF.Square),
                deps=[evr, ev_s2 if dc > 0 else None], sig="s")
            p.step(Tn, lambda t, dc=dc, qc=qc: t.matmul(
                pb[2][0:1, 0:QC], onesc[:, :], xbuf[:, qc % 2, dc, :],
                start=(dc == 0), stop=(dc == DC - 1)),
                deps=[evr, ev_oc] + ([prev_stat_evac] if dc == 0 else []), sig=None)
            ev_s2 = p.step(Tn, lambda t, dc=dc: t.matmul(
                pb[3][0:1, 0:QC], onesc[:, :], sqscr[:, 0:QC],
                start=(dc == 0), stop=(dc == DC - 1)), deps=[evq2], sig="t")
        last_wo = ev_res[-1]
        prev_resid_banks = [ev_res[6], ev_res[7]]
        ev_e1 = p.step(V, lambda v, qc=qc: v.tensor_copy(
            out=rows_stats[0:1, qc * QC:(qc + 1) * QC], in_=pb[2][0:1, 0:QC]),
            deps=[ev_s2], sig="v")
        ev_e2 = p.step(V, lambda v, qc=qc: v.tensor_copy(
            out=rows_stats[0:1, T + qc * QC:T + (qc + 1) * QC],
            in_=pb[3][0:1, 0:QC]), deps=[ev_s2], sig="v")
        ev_spill[qc] = p.dma(G, "qg", X2S[:, :, qc * QC:(qc + 1) * QC],
                             xbuf[:, qc % 2, :, :],
                             deps=[ev_res[-1], ev_e2])
        ev_resid_done[qc] = ev_e2
        prev_stat_evac = ev_e2
        last_scores_cons = ev_res[-1]

    if KPHASE == 31:
        ev_d = p.dma(G, "qg", OUT[0:1, 0, :], rows_stats[0:1, 0:T],
                     deps=[ev_resid_done[-1]])
        ev_d2 = p.dma(G, "qg", OUT[0:1, 1, :], rows_stats[0:1, T:2 * T],
                      deps=[ev_resid_done[-1]])
        p.step(SY, lambda s: s.nop(), deps=[ev_d2], sig=None)
        return
    if KPHASE == 3:
        ev_d = p.dma(G, "qg", OUT.rearrange("p a t -> p (a t)"),
                     X2S.rearrange("p a t -> p (a t)"),
                     deps=[ev_spill[-1], ev_resid_done[-1]])
        p.step(SY, lambda s: s.nop(), deps=[ev_d], sig=None)
        return

    W2v = W2T
    w2evs = []
    W2Qv = W2Q.rearrange("a p n -> p a n")
    for cc in range(CC):
        buf = wraw[:, cc % 2, :]
        deps = [w2evs[-2][0]] if len(w2evs) >= 2 else [ev_resid_done[-1]]
        ev_l = p.dma(SY, "qs", buf, W2v[:, cc, :], deps=deps)
        e1 = p.step(V, lambda v, buf=buf: v.tensor_scalar(
            out=wqf[:, 0:1024], in0=buf, scalar1=colA[:, 3:4],
            scalar2=None, op0=ALU.mult), deps=[ev_l, ev_wsi], sig="v")
        e2 = p.step(V, lambda v: v.tensor_scalar(
            out=wqc[:, 0:1024], in0=wqf[:, 0:1024], scalar1=-0.5,
            scalar2=0.5, op0=ALU.max, op1=ALU.min), deps=[e1], sig="v")
        e3 = p.step(V, lambda v: v.tensor_sub(
            out=wqc[:, 0:1024], in0=wqf[:, 0:1024], in1=wqc[:, 0:1024]),
            deps=[e2], sig="v")
        e4 = p.step(S, lambda s: s.activation(
            out=w2stage[:, :], in_=wqc[:, 0:1024], func=AF.Sign),
            deps=[e3], sig="s")
        e5 = p.dma(SY, "qs",
                   W2Qv[:, :, cc * 128:(cc + 1) * 128],
                   w2stage.rearrange("p (a n) -> p a n", a=DC), deps=[e4])
        w2evs.append((e3, e5))
    ev_w2q_out = w2evs[-1][1]

    W1v = W1T.rearrange("p a c -> p (a c)")
    w1evs = []
    for i in range(32):
        buf = wraw[:, i % 2, :]
        deps = [w1evs[-2][0]] if len(w1evs) >= 2 else [w2evs[-1][0], ev_spill[-1]]
        ev_l = p.dma(SY, "qs", buf, W1v[:, i * 1024:(i + 1) * 1024], deps=deps)
        e1 = p.step(V, lambda v, buf=buf: v.tensor_scalar(
            out=wqf[:, 0:1024], in0=buf, scalar1=colA[:, 2:3],
            scalar2=None, op0=ALU.mult), deps=[ev_l, ev_wsi], sig="v")
        e2 = p.step(V, lambda v: v.tensor_scalar(
            out=wqc[:, 0:1024], in0=wqf[:, 0:1024], scalar1=-0.5,
            scalar2=0.5, op0=ALU.max, op1=ALU.min), deps=[e1], sig="v")
        e3 = p.step(V, lambda v: v.tensor_sub(
            out=wqc[:, 0:1024], in0=wqf[:, 0:1024], in1=wqc[:, 0:1024]),
            deps=[e2], sig="v")
        e4 = p.step(S, lambda s, i=i: s.activation(
            out=w1q.rearrange("p a c -> p (a c)")[:, i * 1024:(i + 1) * 1024],
            in_=wqc[:, 0:1024], func=AF.Sign), deps=[e3], sig="s")
        w1evs.append((e3, e4))
    ev_w1q = w1evs[-1][1]

    if KPHASE == 4:
        debug_dump([(c, w1q[:, c, 0:T]) for c in range(DC)], [ev_w1q, ev_w2q_out])
        return

    ev_ln2 = ln_stats_math(ev_resid_done[-1])

    extra = dict(ev_cwb=ev_cwb, ev_cbb=ev_cbb, ev_eye=ev_eye,
                 ev_spill_last=ev_spill[-1], debug_dump=debug_dump)
    ffn(p, b, pb, hbm, ev_ln2, ev_w1q, ev_w2q_out, extra)


def ffn(p, b, pb, hbm, ev_ln2, ev_w1q, ev_w2q_out, extra):
    V, S, Tn, G, SY = "vector", "scalar", "tensor", "gpsimd", "sync"
    ev_cwb, ev_cbb = extra["ev_cwb"], extra["ev_cbb"]
    ev_eye, ev_spill_last = extra["ev_eye"], extra["ev_spill_last"]
    debug_dump = extra["debug_dump"]
    X2S, W2Q, OUT = hbm["X2S"], hbm["W2Q"], hbm["OUT"]
    rows_stats, rows_tmp, srow = b["rows_stats"], b["rows_tmp"], b["srow"]
    qisc, eyef, colA = b["qisc"], b["eyef"], b["colA"]
    ones128, cw_s, cb_s = b["ones128"], b["cw_s"], b["cb_s"]
    w1q, hbuf, x2a, xn2 = b["w1q"], b["hbuf"], b["x2a"], b["xn2"]
    fscr, xq2, xqs, w2qb = b["fscr"], b["xq2"], b["xqs"], b["w2qb"]
    bcB, outb, tscr, cscr, mxr = (b["bcB"], b["outb"], b["tscr"],
                                  b["cscr"], b["mxr"])
    bcC = b["bcC"]

    ev_out = []
    ev_xq2 = [None, None]
    ev_w2_done = None
    ev_hfree = None
    prev_pb4 = None

    def quant_rounds(src3, n_chunks, zslice_of, dst_bf, bcs1, ev_deps, tag):
        evs_all = []
        nround = n_chunks // 4
        for r in range(nround):
            evz = []
            for k in range(4):
                c = r * 4 + k
                evz.append(p.step(V, lambda v, c=c, k=k: v.tensor_tensor(
                    out=fscr[:, k, 0:zslice_of], in0=src3(c),
                    in1=bcs1[:, 0:zslice_of], op=ALU.mult),
                    deps=ev_deps + (evs_all[-1:] if evs_all else []), sig="v"))
            zap = fscr[:, 0:4, 0:zslice_of]
            eap = fscr[:, 4:8, 0:zslice_of]
            qap = qisc[:, 0:4, 0:zslice_of]
            eA = p.step(V, lambda v, qap=qap, zap=zap: v.tensor_scalar(
                out=qap, in0=zap, scalar1=MAGIC, scalar2=MAGIC,
                op0=ALU.add, op1=ALU.subtract), deps=[evz[-1]], sig="v")
            eC = p.step(V, lambda v, qap=qap, zap=zap, eap=eap: v.scalar_tensor_tensor(
                out=eap, in0=qap, scalar=0.0, in1=zap,
                op0=ALU.add, op1=ALU.subtract), deps=[eA], sig="v")
            eD = p.step(V, lambda v, qap=qap, zap=zap, eap=eap: v.scalar_tensor_tensor(
                out=zap, in0=eap, scalar=-0.5, in1=qap,
                op0=ALU.is_lt, op1=ALU.add), deps=[eC], sig="v")
            eE = p.step(V, lambda v, r=r, zap=zap, eap=eap: v.scalar_tensor_tensor(
                out=dst_bf(r), in0=eap, scalar=0.5, in1=zap,
                op0=ALU.is_le, op1=ALU.add), deps=[eD], sig="v")
            eE2 = p.step(V, lambda v, r=r: v.tensor_scalar(
                out=dst_bf(r), in0=dst_bf(r), scalar1=-1.0, scalar2=None,
                op0=ALU.add), deps=[eE], sig="v")
            evs_all.append(eE2)
        return evs_all[-1]

    def tree_and_scales(src3, n_chunks, width, srow_off, ev_deps):
        evs = []
        nacc = 8 if n_chunks >= 16 else n_chunks // 2
        for which, op in ((0, ALU.max), (1, ALU.min)):
            e = p.step(V, lambda v, op=op: v.tensor_tensor(
                out=fscr[:, 0:nacc, 0:width], in0=src3(0, nacc, width),
                in1=src3(nacc, 2 * nacc, width), op=op),
                deps=ev_deps + (evs[-1:] if evs else []), sig="v")
            for g in range(2, n_chunks // nacc):
                e = p.step(V, lambda v, g=g, op=op: v.tensor_tensor(
                    out=fscr[:, 0:nacc, 0:width], in0=fscr[:, 0:nacc, 0:width],
                    in1=src3(g * nacc, (g + 1) * nacc, width), op=op),
                    deps=[e], sig="v")
            n = nacc
            while n > 1:
                m = n // 2
                e = p.step(V, lambda v, m=m, n=n, op=op: v.tensor_tensor(
                    out=fscr[:, 0:m, 0:width], in0=fscr[:, 0:m, 0:width],
                    in1=fscr[:, m:n, 0:width], op=op), deps=[e], sig="v")
                n = m
            if which == 0:
                e = p.step(V, lambda v: v.tensor_copy(
                    out=mxr[:, 0:width], in_=fscr[:, 0, 0:width]),
                    deps=[e], sig="v")
            evs.append(e)
        blocks = []
        nb = (width + 127) // 128
        for i in range(nb):
            st = min(i * 128, width - 128)
            blocks.append((st, st + 128))
        ev_rows = []
        for which in range(2):
            srcm = (lambda s0: mxr[:, s0:s0 + 128]) if which == 0 else \
                   (lambda s0: fscr[:, 0, s0:s0 + 128])
            for (s0, s1) in blocks:
                et = p.step(Tn, lambda t, srcm=srcm, s0=s0: t.transpose(
                    out=pb[6][:, 0:128], in_=srcm(s0),
                    identity=eyef[:, :]),
                    deps=[evs[1], ev_eye] + ev_rows[-1:], sig="t")
                er = p.step(V, lambda v, which=which: (
                    v.reduce_max(out=colA[:, 12:13], in_=pb[6][:, 0:128], axis=AX.X)
                    if which == 0 else
                    v.tensor_reduce(out=colA[:, 12:13], in_=pb[6][:, 0:128],
                                    axis=AX.X, op=ALU.min)), deps=[et], sig="v")
                et2 = p.step(Tn, lambda t: t.transpose(
                    out=pb[7][0:1, 0:128], in_=colA[:, 12:13],
                    identity=eyef[:, :]), deps=[er], sig="t")
                base = 640 if which == 0 else 960
                er2 = p.step(V, lambda v, base=base, s0=s0, s1=s1:
                             v.tensor_copy(
                    out=srow[0:1, base + s0:base + s1],
                    in_=pb[7][0:1, 0:s1 - s0]), deps=[et2], sig="v")
                ev_rows.append(er2)
        e1 = p.step(V, lambda v: v.tensor_scalar(
            out=srow[0:1, 960:960 + width], in0=srow[0:1, 960:960 + width],
            scalar1=-1.0, scalar2=None, op0=ALU.mult), deps=[ev_rows[-1]], sig="v")
        e2 = p.step(V, lambda v: v.tensor_tensor(
            out=srow[0:1, 640:640 + width], in0=srow[0:1, 640:640 + width],
            in1=srow[0:1, 960:960 + width], op=ALU.max), deps=[e1], sig="v")
        e3 = p.step(V, lambda v: v.tensor_scalar(
            out=srow[0:1, 640:640 + width], in0=srow[0:1, 640:640 + width],
            scalar1=1e-5, scalar2=None, op0=ALU.max), deps=[e2], sig="v")
        e4 = p.step(V, lambda v: v.reciprocal(
            out=srow[0:1, srow_off:srow_off + width],
            in_=srow[0:1, 640:640 + width]), deps=[e3], sig="v")
        e5 = p.step(V, lambda v: v.tensor_scalar(
            out=srow[0:1, srow_off:srow_off + width],
            in0=srow[0:1, srow_off:srow_off + width],
            scalar1=127.0, scalar2=None, op0=ALU.mult), deps=[e4], sig="v")
        e6 = p.step(V, lambda v: v.reciprocal(
            out=srow[0:1, 0:width],
            in_=srow[0:1, srow_off:srow_off + width]), deps=[e5], sig="v")
        return e5, e6

    last_h_mm = None
    stop = False
    for pair in range(NSTRIP // 2):
        if stop:
            break
        for half in range(2):
            if stop:
                break
            i = pair * 2 + half
            t0 = i * STRIP
            lo = max(t0 - 1, 0)
            hi = min(t0 + STRIP + 1, T)
            off = 1 if t0 == 0 else 0
            ev_x2 = p.dma(G, "qg", x2a[:, :, off:off + (hi - lo)],
                          X2S[:, :, lo:hi],
                          deps=[ev_ln2, ev_spill_last, ev_hfree])
            if t0 == 0:
                p.step(V, lambda v: v.memset(x2a[:, :, 0:1], 0.0), sig="v")
            if hi == T:
                p.step(V, lambda v: v.memset(x2a[:, :, SL - 1:SL], 0.0), sig="v")
            if KPHASE == 821 and i == 1:
                debug_dump([(OUT.rearrange("p a t -> p (a t)")[:, 0:CC * 2 * STRIP],
                             xq2.rearrange("p a t -> p (a t)"))], [ev_hfree, ev_x2])
                stop = True
                break
            eb1 = p.step(Tn, lambda t, lo=lo, hi=hi: t.matmul(
                pb[4][:, 0:hi - lo], ones128[:, :], rows_stats[0:1, lo:hi],
                start=True, stop=True),
                deps=[ev_ln2, ev_hfree, ev_w2_done, prev_pb4, ev_w2q_out,
                      ev_out[-1] if ev_out else None], sig="t")
            eb1e = p.step(S, lambda s, lo=lo, hi=hi, off=off: s.copy(
                out=cscr[:, 0, off:off + hi - lo], in_=pb[4][:, 0:hi - lo]),
                deps=[eb1], sig="s")
            eb2 = p.step(Tn, lambda t, lo=lo, hi=hi: t.matmul(
                pb[5][:, 0:hi - lo], ones128[:, :], rows_stats[0:1, T + lo:T + hi],
                start=True, stop=True), deps=[ev_ln2, ev_hfree], sig="t")
            eb2e = p.step(S, lambda s, lo=lo, hi=hi, off=off: s.copy(
                out=bcB[:, off:off + hi - lo], in_=pb[5][:, 0:hi - lo]),
                deps=[eb2], sig="s")
            if KPHASE == 822 and i == 1:
                debug_dump([(OUT.rearrange("p a t -> p (a t)")[:, 0:CC * 2 * STRIP],
                             xq2.rearrange("p a t -> p (a t)"))], [ev_hfree, eb1e, eb2e])
                stop = True
                break
            ev_ap = []
            for dc in range(DC):
                e1 = p.step(V, lambda v, dc=dc: v.tensor_sub(
                    out=xn2[:, dc, :], in0=x2a[:, dc, :], in1=cscr[:, 0, :]),
                    deps=[ev_x2, eb1e], sig="v")
                e2 = p.step(V, lambda v, dc=dc: v.tensor_tensor(
                    out=xn2[:, dc, :], in0=xn2[:, dc, :], in1=bcB[:, :],
                    op=ALU.mult), deps=[e1, eb2e], sig="v")
                ev_ap.append(e2)
            if KPHASE == 82 and i == 1:
                debug_dump([(OUT.rearrange("p a t -> p (a t)")[:, 0:CC * 2 * STRIP],
                             xq2.rearrange("p a t -> p (a t)"))], [ev_ap[-1]])
                stop = True
                break
            ev_s, ev_sdq = tree_and_scales(
                lambda a, b_, w: xn2[:, a:b_, 0:w], DC, SL, 320, [ev_ap[-1]])
            eb3 = p.step(Tn, lambda t: t.matmul(
                pb[4][:, 0:SL], ones128[:, :], srow[0:1, 320:320 + SL],
                start=True, stop=True), deps=[ev_s, ev_ap[-1], eb1e], sig="t")
            eb3e = p.step(S, lambda s: s.copy(
                out=bcB[:, :], in_=pb[4][:, 0:SL]), deps=[eb3], sig="s")
            ev_sdq2 = p.step(V, lambda v, half=half: v.tensor_copy(
                out=rows_tmp[0:1, half * SL:(half + 1) * SL],
                in_=srow[0:1, 0:SL]), deps=[ev_sdq], sig="v")
            ev_xq = quant_rounds(
                lambda c: xn2[:, c, :], DC, SL,
                lambda r: xqs[:, r * 4:(r + 1) * 4, :], bcB,
                [eb3e, ev_sdq2], "xq")
            if t0 == 0:
                ev_xq = p.step(V, lambda v: v.memset(xqs[:, :, 0:1], 0.0),
                               deps=[ev_xq], sig="v")
            if t0 + STRIP == T:
                ev_xq = p.step(V, lambda v: v.memset(xqs[:, :, SL - 1:SL], 0.0),
                               deps=[ev_xq], sig="v")
            if KPHASE == 5 and i == 0:
                debug_dump([(OUT[:, 0:2, :].rearrange("p a t -> p (a t)")[:, 0:DC * SL],
                             xn2.rearrange("p a t -> p (a t)"))],
                           [ev_xq] + ([] if t0 else []))
                debug_dump([(OUT[:, 2:4, :].rearrange("p a t -> p (a t)")[:, 0:DC * SL],
                             xqs.rearrange("p a t -> p (a t)"))], [ev_xq])
                return ("STOP",)
            eb4 = p.step(Tn, lambda t, half=half: t.matmul(
                pb[4][:, 0:SL], ones128[:, :],
                rows_tmp[0:1, half * SL:(half + 1) * SL],
                start=True, stop=True), deps=[ev_sdq2, ev_xq, eb3e], sig="t")
            eb4e = p.step(S, lambda s: s.copy(
                out=bcB[:, :], in_=pb[4][:, 0:SL]), deps=[eb4], sig="s")

            ev_pair_q = []
            ev_sil = []
            for cc in range(CC):
                bank = cc % 2
                for dc in range(DC):
                    evm = p.step(Tn, lambda t, cc=cc, dc=dc, bank=bank: t.matmul(
                        pb[bank][:, 0:SL], w1q[:, dc, cc * 128:(cc + 1) * 128],
                        xqs[:, dc, :], start=(dc == 0), stop=(dc == DC - 1)),
                        deps=[ev_xq, ev_w1q, ev_hfree,
                              ev_sil[cc - 2] if cc >= 2 else None], sig="t")
                e1 = p.step(V, lambda v, cc=cc, bank=bank: v.scalar_tensor_tensor(
                    out=hbuf[:, cc, :], in0=pb[bank][:, 0:SL],
                    scalar=colA[:, 6:7], in1=bcB[:, :],
                    op0=ALU.mult, op1=ALU.mult), deps=[evm, eb4e], sig="v")
                par = cc % 2
                e2 = p.step(V, lambda v, cc=cc, par=par: v.tensor_scalar(
                    out=cscr[:, par, 1:SL - 1], in0=hbuf[:, cc, 1:SL - 1],
                    scalar1=cw_s[:, cc, 1:2], scalar2=cb_s[:, cc:cc + 1],
                    op0=ALU.mult, op1=ALU.add),
                    deps=[e1, ev_cwb, ev_cbb,
                          ev_sil[cc - 2] if cc >= 2 else None], sig="v")
                e3 = p.step(V, lambda v, cc=cc, par=par: v.scalar_tensor_tensor(
                    out=cscr[:, par, 1:SL - 1], in0=hbuf[:, cc, 0:SL - 2],
                    scalar=cw_s[:, cc, 0:1], in1=cscr[:, par, 1:SL - 1],
                    op0=ALU.mult, op1=ALU.add), deps=[e2], sig="v")
                e4 = p.step(V, lambda v, cc=cc, par=par: v.scalar_tensor_tensor(
                    out=cscr[:, par, 1:SL - 1], in0=hbuf[:, cc, 2:SL],
                    scalar=cw_s[:, cc, 2:3], in1=cscr[:, par, 1:SL - 1],
                    op0=ALU.mult, op1=ALU.add), deps=[e3], sig="v")
                e5 = p.step(S, lambda s, cc=cc, par=par: s.activation(
                    out=hbuf[:, cc, 1:SL - 1], in_=cscr[:, par, 1:SL - 1],
                    func=AF.Silu), deps=[e4], sig="s")
                ev_sil.append(e5)
            if KPHASE == 6 and i == 0:
                debug_dump([(OUT.rearrange("p a t -> p (a t)")[:, 0:CC * SL],
                             hbuf.rearrange("p a t -> p (a t)"))], [ev_sil[-1]])
                stop = True
                break
            if KPHASE == 84 and i == 1:
                debug_dump([(OUT.rearrange("p a t -> p (a t)")[:, 0:CC * 2 * STRIP],
                             xq2.rearrange("p a t -> p (a t)"))], [ev_sil[-1]])
                stop = True
                break
            ev_s2, ev_sdq2b = tree_and_scales(
                lambda a, b_, w: hbuf[:, a:b_, 1:1 + w], CC, STRIP, 320,
                [ev_sil[-1]])
            eb5 = p.step(Tn, lambda t: t.matmul(
                pb[4][:, 0:STRIP], ones128[:, :], srow[0:1, 320:320 + STRIP],
                start=True, stop=True), deps=[ev_s2, eb4e], sig="t")
            eb5e = p.step(S, lambda s: s.copy(
                out=bcB[:, 0:STRIP], in_=pb[4][:, 0:STRIP]), deps=[eb5], sig="s")
            prev_pb4 = eb5e
            ev_sd2 = p.step(V, lambda v, half=half: v.tensor_copy(
                out=rows_tmp[0:1, 2 * SL + half * STRIP:2 * SL + (half + 1) * STRIP],
                in_=srow[0:1, 0:STRIP]), deps=[ev_sdq2b], sig="v")
            ev_q2 = quant_rounds(
                lambda c: hbuf[:, c, 1:1 + STRIP], CC, STRIP,
                lambda r, half=half: xq2[:, r * 4:(r + 1) * 4,
                                         half * STRIP:(half + 1) * STRIP],
                bcB, [eb5e, ev_sd2], "q2")
            ev_xq2[half] = ev_q2
            ev_hfree = ev_q2
            if KPHASE == 824 and i == 0:
                ev_t = p.dma(G, "qg", x2a[:, :, 0:258], X2S[:, :, 255:513],
                             deps=[ev_q2])
                debug_dump([(OUT.rearrange("p a t -> p (a t)")[:, 0:CC * 2 * STRIP],
                             xq2.rearrange("p a t -> p (a t)"))], [ev_q2, ev_t])
                stop = True
                break
            if KPHASE == 81 and i == 0:
                debug_dump([(OUT.rearrange("p a t -> p (a t)")[:, 0:CC * 2 * STRIP],
                             xq2.rearrange("p a t -> p (a t)"))], [ev_q2])
                stop = True
                break
            if KPHASE == 8 and i == 1:
                debug_dump([(OUT.rearrange("p a t -> p (a t)")[:, 0:CC * 2 * STRIP],
                             xq2.rearrange("p a t -> p (a t)"))], [ev_q2])
                stop = True
                break

        if stop:
            break
        W2Qr = W2Q
        ev_res_pair = []
        ev_mm_prev = [None, None]
        eb6e = None
        for dc in range(DC):
            bank = 2 + dc % 2
            evm = None
            for half in range(2):
                bufq = (dc * 2 + half) % 2
                ev_wl = p.dma(SY, "qs",
                              w2qb[:, bufq, :, :],
                              W2Qr[dc, :, half * 2048:(half + 1) * 2048],
                              deps=[ev_w2q_out, ev_xq2[1], ev_mm_prev[bufq]])
                for ci in range(16):
                    cc = half * 16 + ci
                    evm = p.step(Tn, lambda t, dc=dc, cc=cc, ci=ci, bank=bank,
                                 bufq=bufq: t.matmul(
                        pb[bank][:, 0:512], w2qb[:, bufq, ci, :],
                        xq2[:, cc, :], start=(cc == 0), stop=(cc == CC - 1)),
                        deps=[ev_wl, ev_xq2[1],
                              ev_res_pair[dc - 2] if dc >= 2 else None], sig="t")
                ev_mm_prev[bufq] = evm
            if dc == 0:
                eb6 = p.step(Tn, lambda t, pair=pair: t.matmul(
                    pb[4][:, 0:512], ones128[:, :],
                    rows_tmp[0:1, 2 * SL:2 * SL + 512],
                    start=True, stop=True), deps=[ev_xq2[1], prev_pb4], sig="t")
                eb6e = p.step(S, lambda s: s.copy(
                    out=bcC[:, :], in_=pb[4][:, 0:512]), deps=[eb6], sig="s")
                prev_pb4 = eb6e
            ev_xr = p.dma(G, "qg", outb[:, dc % 2, :],
                          X2S[:, dc, pair * 512:(pair + 1) * 512],
                          deps=[ev_out[-1] if ev_out else None, ev_spill_last,
                                ev_xq2[1]])
            e1 = p.step(V, lambda v, bank=bank: v.scalar_tensor_tensor(
                out=tscr[:, :], in0=pb[bank][:, 0:512], scalar=colA[:, 7:8],
                in1=bcC[:, :], op0=ALU.mult, op1=ALU.mult),
                deps=[evm, eb6e], sig="v")
            e2 = p.step(V, lambda v, dc=dc: v.tensor_tensor(
                out=outb[:, dc % 2, :], in0=tscr[:, :], in1=outb[:, dc % 2, :],
                op=ALU.add), deps=[e1, ev_xr], sig="v")
            ev_res_pair.append(e2)
            ev_o = p.dma(G, "qg", OUT[:, dc, pair * 512:(pair + 1) * 512],
                         outb[:, dc % 2, :], deps=[e2])
            ev_out.append(ev_o)
        ev_w2_done = ev_res_pair[-1]
        if KPHASE == 7 and pair == 0:
            p.step(SY, lambda s: s.nop(), deps=[ev_out[-1]], sig=None)
            stop = True
            break

    if not stop:
        p.step(SY, lambda s: s.nop(), deps=[ev_out[-1]], sig=None)


_BF = ml_dtypes.bfloat16


def _rope_tables():
    import jax
    import jax.numpy as jnp
    with jax.default_device(jax.devices("cpu")[0]):
        positions = jnp.arange(0, HS, 2, dtype=jnp.float32)
        theta = 10000.0 ** (-2.0 * positions / HS)
        angles = jnp.arange(T, dtype=jnp.float32)[:, None] * theta[None, :]
        cos32 = np.asarray(jnp.cos(angles))
        sin32 = np.asarray(jnp.sin(angles))
    cos_t = np.ascontiguousarray(cos32.T)
    sin_t = np.ascontiguousarray(sin32.T)
    cos_fm = np.concatenate([cos_t, cos_t], axis=0).astype(np.float32)
    sin_fm = np.concatenate([-sin_t, sin_t], axis=0).astype(np.float32)
    return cos_fm, sin_fm


def _chunked(a, nchunk):
    n = a.shape[0] // 128
    assert n == nchunk
    return np.ascontiguousarray(a.reshape(nchunk, 128, -1).transpose(1, 0, 2))


_NC_CACHE = {}


def kernel(L0_attn_x, L0_attn_Wq, L0_attn_Wk, L0_attn_Wv, L0_attn_Wo,
           L0_ffn_w1, L0_ffn_conv_w, L0_ffn_conv_b, L0_ffn_w2):
    x = np.asarray(L0_attn_x)
    perm = np.concatenate([np.arange(0, 128, 2), np.arange(1, 128, 2)])
    wq = np.asarray(L0_attn_Wq)[:, perm, :]
    wk = np.asarray(L0_attn_Wk)[perm, :]
    wv = np.asarray(L0_attn_Wv)
    wo = np.asarray(L0_attn_Wo)
    w1 = np.asarray(L0_ffn_w1)
    cw = np.asarray(L0_ffn_conv_w)
    cb = np.asarray(L0_ffn_conv_b)
    w2 = np.asarray(L0_ffn_w2)

    cos_fm, sin_fm = _rope_tables()
    cosin = np.ascontiguousarray(np.stack([cos_fm, sin_fm], axis=1))

    msk = np.zeros((128, 2, QC), np.float32)
    pidx = np.arange(128)[:, None]
    tidx = np.arange(QC)[None, :]
    for ai, a in enumerate((0, 128)):
        msk[:, ai, :] = np.where(tidx >= a + pidx, 0.0, NEG)
    msk = msk.astype(_BF)

    eye = np.eye(128, dtype=np.float32)
    rot = np.zeros((128, 128), np.float32)
    m = np.arange(128)
    rot[(m + 64) % 128, m] = 1.0
    eyer = np.ascontiguousarray(np.stack([eye, rot], axis=1))

    wq_h = np.ascontiguousarray(np.stack(
        [_chunked(wq[h].T, DC) for h in range(2)], axis=1))
    wk_h = _chunked(wk.T, DC)
    wv_h = _chunked(wv.T, DC)
    wo_h = _chunked(wo.T, 2)
    w1t = _chunked(w1.T, DC)
    w2t = _chunked(w2.T, CC)
    cwb = np.concatenate([
        _chunked(cw[:, 0, :], CC).transpose(0, 2, 1).reshape(128, 3 * CC)
        if False else
        np.ascontiguousarray(cw[:, 0, :].reshape(CC, 128, 3)
                             .transpose(1, 0, 2).reshape(128, 3 * CC)),
        cb.reshape(CC, 128).T], axis=1).astype(np.float32)

    in_maps = []
    for c in range(NCORE):
        xt = np.ascontiguousarray(
            x[c].T.reshape(DC, 128, T).transpose(1, 0, 2))
        in_maps.append(dict(
            x=xt,
            wq=np.ascontiguousarray(wq_h.reshape(128, 2 * DC * HS)),
            wk=np.ascontiguousarray(wk_h.reshape(128, DC * HS)),
            wv=np.ascontiguousarray(wv_h.reshape(128, DC * HS)),
            wo=np.ascontiguousarray(wo_h.reshape(128, 2 * D)),
            w1t=w1t, w2t=w2t, cwb=cwb,
            cosin=cosin, msk=msk, eyer=eyer,
        ))

    key = "nc"
    if key not in _NC_CACHE:
        _NC_CACHE[key] = build_program()
    nc = _NC_CACHE[key]
    res = run_bass_kernel_spmd(nc, in_maps, core_ids=list(range(NCORE)))
    outs = []
    for c in range(NCORE):
        o = np.asarray(res.results[c]["out"])
        outs.append(o.transpose(1, 0, 2).reshape(D, T).T)
    return np.stack(outs).astype(np.float32)


if __name__ == "__main__":
    plan_memory()
    print("plan ok")
    nc = build_program()
    print("built:", sum(1 for _ in nc.all_instructions()), "instructions")
